# version 01
# baseline (speedup 1.0000x reference)
import numpy as np
import concourse.bass as bass
import concourse.mybir as mybir
import concourse.bass_isa as bass_isa
from concourse.bass_utils import run_bass_kernel_spmd
from concourse.library_config import attn as _pool_lib

F32 = mybir.dt.float32
BF16 = mybir.dt.bfloat16
I32 = mybir.dt.int32
A = mybir.AluOpType
AF = mybir.ActivationFunctionType
AX = mybir.AxisListType

LN2 = float(np.log(2.0))
S = 2048
D = 1024
H = 16
NB = 2
T = 16
F = 8


class _Eng:
    def __init__(self, name):
        self.name = name
        self.sem = None
        self.n = 0
        self.watermark = {}


class _Prog:

    NDMA = 24

    def __init__(self, blk, stack, nc):
        self.blk = blk
        self.engs = {}
        for name in ("tensor", "vector", "scalar", "gpsimd", "sync"):
            e = _Eng(name)
            e.sem = stack.enter_context(nc.semaphore("sem_" + name))
            self.engs[name] = e
        self.dma_sems = [stack.enter_context(nc.semaphore("dsem%d" % i))
                         for i in range(self.NDMA)]
        self.dma_uses = [0] * self.NDMA
        self.dma_i = 0

    serial = False

    def _op(self, name, deps, fn, inc, sem_override=None):
        en = self.engs[name]
        if self.serial:
            deps = list(deps) + [(e.sem, e.n) for e in self.engs.values()]
            deps += [(self.dma_sems[i], 16 * self.dma_uses[i])
                     for i in range(self.NDMA)]
        need = {}
        for d in deps:
            if d is None:
                continue
            sem, val = d
            if val <= 0:
                continue
            k = id(sem)
            if en.watermark.get(k, 0) >= val:
                continue
            if k not in need or need[k][1] < val:
                need[k] = (sem, val)
        need = list(need.values())
        res = {}

        def body(engine):
            for (sem, val) in need[:-1]:
                engine.wait_ge(sem, val)
                en.watermark[id(sem)] = val
            inst = fn(engine)
            if need:
                sem, val = need[-1]
                inst._wait_ge(sem, val)
                en.watermark[id(sem)] = val
            if sem_override is None:
                en.n += inc
                inst.then_inc(en.sem, inc)
                res["tok"] = (en.sem, en.n)
            else:
                sem, newval = sem_override
                inst.then_inc(sem, inc)
                res["tok"] = (sem, newval)

        getattr(self.blk, name)(body)
        return res["tok"]

    def t(self, deps, fn):
        return self._op("tensor", deps, fn, 1)

    def v(self, deps, fn):
        return self._op("vector", deps, fn, 1)

    def s(self, deps, fn):
        return self._op("scalar", deps, fn, 1)

    def g(self, deps, fn):
        return self._op("gpsimd", deps, fn, 1)

    def y(self, deps, fn):
        i = self.dma_i % self.NDMA
        self.dma_i += 1
        sem = self.dma_sems[i]
        u = self.dma_uses[i]
        self.dma_uses[i] = u + 1
        deps = list(deps) + [(sem, 16 * u)]
        return self._op("sync", deps, fn, 16,
                        sem_override=(sem, 16 * (u + 1)))


class _Buf:
    def __init__(self):
        self.w = None
        self.r = []

    def rd(self):
        return [self.w] if self.w else []

    def wr(self):
        return ([self.w] if self.w else []) + self.r

    def did_read(self, tok):
        self.r.append(tok)
        if len(self.r) > 600:
            self.r = self.r[-600:]

    def did_write(self, tok):
        self.w = tok
        self.r = []


def build_program(taps=(), phases=5, serial=False):
    nc = bass.Bass("TRN2", target_bir_lowering=False)

    import os as _os
    nonce = "nonce_" + _os.urandom(6).hex()
    nc.dram_tensor(nonce, [1, 1], F32, kind="ExternalInput")

    x_in = nc.dram_tensor("x", [NB * S, D], F32, kind="ExternalInput").ap()
    wqkv = nc.dram_tensor("w_qkv", [3 * D, D], F32, kind="ExternalInput").ap()
    bqkv = nc.dram_tensor("b_qkv", [3 * D], F32, kind="ExternalInput").ap()
    wout = nc.dram_tensor("w_out", [D, D], F32, kind="ExternalInput").ap()
    bout = nc.dram_tensor("b_out", [D], F32, kind="ExternalInput").ap()
    wffn = nc.dram_tensor("w_ffn", [D, D], F32, kind="ExternalInput").ap()
    bffn = nc.dram_tensor("b_ffn", [D], F32, kind="ExternalInput").ap()
    out_d = nc.dram_tensor("out", [NB * S, D], F32, kind="ExternalOutput").ap()

    wqT_hi = nc.dram_tensor("wqT_hi", [F, 128, 3 * D], BF16).ap()
    wqT_lo = nc.dram_tensor("wqT_lo", [F, 128, 3 * D], BF16).ap()
    woT_hi = nc.dram_tensor("woT_hi", [F, 128, D], BF16).ap()
    woT_lo = nc.dram_tensor("woT_lo", [F, 128, D], BF16).ap()
    wfT_d = nc.dram_tensor("wfT_d", [F, 128, D], BF16).ap()
    ct_dhi = nc.dram_tensor("ct_dhi", [F, 128, S], BF16).ap()
    ct_dlo = nc.dram_tensor("ct_dlo", [F, 128, S], BF16).ap()

    tap_aps = {}

    def tap(name, src_fn, deps, buf, shape, dtype=F32):
        if name not in taps:
            return
        bufs = buf if isinstance(buf, list) else ([buf] if buf else [])
        ap = nc.dram_tensor("tap_" + name, shape, dtype,
                            kind="ExternalOutput").ap()
        tap_aps[name] = ap
        tk = P.y(deps + [t for bf in bufs for t in bf.rd()],
                 lambda y: y.dma_start(out=ap, in_=src_fn()))
        for bf in bufs:
            bf.did_read(tk)
        ftoks.append(tk)

    from contextlib import ExitStack
    with ExitStack() as st:
        en = st.enter_context

        ident = en(nc.sbuf_tensor("ident", [128, 128], F32))
        wfs = en(nc.sbuf_tensor("wfs", [128, D], BF16))
        consts = en(nc.sbuf_tensor("consts", [128, 8], F32))
        ones_bf = en(nc.sbuf_tensor("ones_bf", [128, 1], BF16))
        bq_col = en(nc.sbuf_tensor("bq_col", [128, 16], F32))
        bv_b = en(nc.sbuf_tensor("bv_b", [128, D], F32))
        bo_b = bv_b
        bf_b = en(nc.sbuf_tensor("bf_b", [128, D], F32))
        nrow = en(nc.sbuf_tensor("nrow", [1, D], F32))

        xlnT_hi = en(nc.sbuf_tensor("xlnT_hi", [128, F * S], BF16))
        xlnT_lo = en(nc.sbuf_tensor("xlnT_lo", [128, F * S], BF16))
        vws = en(nc.sbuf_tensor("vws", [128, 2 * 2 * D], BF16))
        qkw_hi = en(nc.sbuf_tensor("qkw_hi", [128, 2 * F * 128], BF16))
        qkw_lo = en(nc.sbuf_tensor("qkw_lo", [128, 2 * F * 128], BF16))
        QTh = en(nc.sbuf_tensor("QTh", [128, 2 * S], BF16))
        QTl = en(nc.sbuf_tensor("QTl", [128, 2 * S], BF16))
        KTh = en(nc.sbuf_tensor("KTh", [128, 2 * S], BF16))
        KTl = en(nc.sbuf_tensor("KTl", [128, 2 * S], BF16))
        Vt = en(nc.sbuf_tensor("Vt", [128, 4 * H * 65], BF16))
        expt = en(nc.sbuf_tensor("expt", [128, 4 * 512], F32))
        eh = en(nc.sbuf_tensor("eh", [128, 4 * 512], BF16))
        el = en(nc.sbuf_tensor("el", [128, 4 * 512], BF16))
        cn_s = en(nc.sbuf_tensor("cn_s", [128, 512], F32))
        rcb = en(nc.sbuf_tensor("rcb", [64, 512], F32))
        nrm = en(nc.sbuf_tensor("nrm", [128, 512], BF16))
        onesr = en(nc.sbuf_tensor("onesr", [128, 128], F32))
        cscal = en(nc.sbuf_tensor("cscal", [1, 16], F32))
        negc = en(nc.sbuf_tensor("negc", [128, 1], F32))
        stab = en(nc.sbuf_tensor("stab", [128, 4], F32))
        cts_hi = en(nc.sbuf_tensor("cts_hi", [128, S], BF16))
        cts_lo = en(nc.sbuf_tensor("cts_lo", [128, S], BF16))

        scr = [en(nc.sbuf_tensor("scr%d" % i, [128, D], F32)) for i in range(6)]
        stg_hi = QTh[:, 0:F * 128]
        stg_lo = QTh[:, F * 128:2 * F * 128]
        iti = en(nc.sbuf_tensor("iti", [128, D], I32))
        colt = en(nc.sbuf_tensor("colt", [128, 8], F32))
        x_pp = en(nc.sbuf_tensor("x_pp", [128, D], F32))
        ct_s = en(nc.sbuf_tensor("ct_s", [128, 2 * 512], BF16))
        wo_s = en(nc.sbuf_tensor("wo_s", [128, 2 * D], BF16))
        x2qT = en(nc.sbuf_tensor("x2qT", [128, F * 128], BF16))

        ps = en(nc.psum_tensor("ps", [128, 8 * 512], F32))

        def bank(i, w=512, p0=0, pn=128):
            return ps[p0:p0 + pn, 512 * i:512 * i + w]

        P = _Prog(en(nc.Block()), st, nc)
        P.serial = serial
        ftoks = []

        pbuf = [_Buf() for _ in range(8)]
        b_xlnT = _Buf()
        b_qkw = [_Buf(), _Buf()]
        b_QT = [_Buf(), _Buf()]
        b_KT = [_Buf(), _Buf()]
        b_V = [_Buf() for _ in range(4)]
        b_expt = [_Buf() for _ in range(4)]
        b_eh = [_Buf() for _ in range(4)]
        b_el = [_Buf() for _ in range(4)]
        b_cts = _Buf()
        b_scr = [_Buf() for _ in range(6)]
        b_stg = _Buf()
        b_xpp = [_Buf()]
        b_x1 = _Buf()
        b_cn = _Buf()
        b_rcb = _Buf()
        b_nrm = _Buf()
        b_iti = _Buf()
        b_colt = _Buf()
        b_x2qT = _Buf()
        b_cscal = _Buf()
        b_negc = _Buf()
        b_stab = _Buf()
        b_vw = _Buf()
        b_wfT = _Buf()
        b_ctd = _Buf()
        b_cs = [_Buf(), _Buf()]
        b_ws = [_Buf(), _Buf()]
        b_wfs = [_Buf(), _Buf()]
        b_vw2 = [_Buf(), _Buf()]
        b_nrow = _Buf()

        P.blk.gpsimd(lambda g: g.load_library(_pool_lib))
        t0 = P.g([], lambda g: g.memset(ident[:], 0.0))
        tident = P.g([t0], lambda g: g.affine_select(
            out=ident[:], in_=ident[:], compare_op=A.not_equal, fill=1.0,
            base=0, pattern=[[-1, 128]], channel_multiplier=1))
        P.g([], lambda g: g.memset(consts[:, 0:1], 1e-5))
        P.g([], lambda g: g.memset(consts[:, 1:2], 2.0 ** -60))
        tones = P.g([], lambda g: g.memset(consts[:, 2:3], 1.0))
        tones_bf = P.g([], lambda g: g.memset(ones_bf[:], 1.0))
        tnegc0 = P.v(b_negc.wr(), lambda v: v.memset(negc[:], 0.0))
        b_negc.did_write(tnegc0)
        tonesr = P.v([], lambda v: v.memset(onesr[:], 1.0))

        vb_src = bass.AP(bqkv.tensor, 128,
                         [[1, 1], [384, 8], [192, 2], [1, 64]])
        vb_dst = nrow[0:1, :].rearrange("a (j g r) -> a j g r", j=8, g=2, r=64)
        tvrow = P.y([], lambda y: y.dma_start(out=vb_dst, in_=vb_src))
        tbv = P.g([tvrow], lambda g: g.partition_broadcast(bv_b[:], nrow[0:1, :]))
        b_nrow.did_write(tvrow)
        b_nrow.did_read(tbv)
        for j in range(16):
            sec, jj = divmod(j, 8)
            src = bass.AP(bqkv.tensor, 384 * jj + 64 * sec,
                          [[192, 2], [1, 64], [1, 1]])
            P.y([], lambda y, s_=src, d_=bq_col[:, j:j + 1]:
                y.dma_start(out=d_, in_=s_))
        tbf_row = P.y(b_nrow.wr(), lambda y: y.dma_start(
            out=nrow[0:1, :], in_=bass.AP(bffn.tensor, 0, [[1, 1], [1, D]])))
        tbf = P.g([tbf_row], lambda g: g.partition_broadcast(bf_b[:], nrow[0:1, :]))
        b_nrow.did_write(tbf_row)
        b_nrow.did_read(tbf)
        b_bvb = _Buf()
        b_bvb.did_write(tbv)

        def wprep(src_fn, n_chunks, dhi, dlo, ncols):
            done = []
            for j in range(n_chunks):
                tld = P.y(b_scr[5].wr(), lambda y, j=j: y.dma_start(
                    out=scr[5][:], in_=src_fn(j)))
                b_scr[5].did_write(tld)
                last = None
                for f in range(F):
                    bk = 6 + (f % 2)
                    tt = P.t([tld, tident] + pbuf[bk].wr(),
                             lambda t, f=f, bk=bk: t.transpose(
                                 bank(bk, 128), scr[5][:, 128 * f:128 * (f + 1)],
                                 ident[:]))
                    pbuf[bk].did_write(tt)
                    b_scr[5].did_read(tt)
                    th = P.s([tt] + b_stg.wr(), lambda s, f=f, bk=bk: s.activation(
                        out=stg_hi[:, 128 * f:128 * (f + 1)], in_=bank(bk, 128),
                        func=AF.Copy))
                    tl = P.v([tt, th], lambda v, f=f, bk=bk: v.tensor_tensor(
                        out=stg_lo[:, 128 * f:128 * (f + 1)], in0=bank(bk, 128),
                        in1=stg_hi[:, 128 * f:128 * (f + 1)], op=A.subtract))
                    pbuf[bk].did_read(tl)
                    b_stg.did_write(tl)
                    last = tl
                hi_ap = bass.AP(dhi.tensor, 128 * j,
                                [[ncols, 128], [128 * ncols, F], [1, 128]])
                lo_ap = bass.AP(dlo.tensor, 128 * j,
                                [[ncols, 128], [128 * ncols, F], [1, 128]])
                s_hi = stg_hi
                s_lo = stg_lo
                t1 = P.y([last], lambda y, a=hi_ap, s_=s_hi:
                         y.dma_start(out=a, in_=s_))
                t2 = P.y([last], lambda y, a=lo_ap, s_=s_lo:
                         y.dma_start(out=a, in_=s_))
                b_stg.did_read(t1)
                b_stg.did_read(t2)
                done.append(t1)
                done.append(t2)
            return done

        def wq_src(j):
            sec, jj = divmod(j, 8)
            return bass.AP(wqkv.tensor, (384 * jj + 64 * sec) * D,
                           [[192 * D, 2], [D, 64], [1, D]])

        def wo_src(j):
            return bass.AP(wout.tensor, 128 * j * D, [[D, 128], [1, D]])

        twq = wprep(wq_src, 24, wqT_hi, wqT_lo, 3 * D)
        two = wprep(wo_src, 8, woT_hi, woT_lo, D)

        def q_grid(src_ap, dst_ap, p_bits, emin, deps, dst_buf):
            d0 = list(deps) + b_scr[0].wr()
            t1 = P.s(d0, lambda s: s.activation(
                out=scr[0][:], in_=src_ap, func=AF.Square))
            b_scr[0].did_write(t1)
            t2 = P.s([t1] + b_scr[1].wr(), lambda s: s.activation(
                out=scr[1][:], in_=scr[0][:], func=AF.Ln, bias=consts[:, 1:2]))
            b_scr[1].did_write(t2)
            t3 = P.v([t2] + b_scr[0].wr(), lambda v: v.tensor_scalar(
                out=scr[0][:], in0=scr[1][:], scalar1=0.5 / LN2, scalar2=None,
                op0=A.mult))
            b_scr[0].did_write(t3)
            b_scr[1].did_read(t3)
            t4 = P.v([t3] + b_iti.wr(), lambda v: v.tensor_copy(
                out=iti[:], in_=scr[0][:]))
            b_iti.did_write(t4)
            t5 = P.v([t4] + b_scr[1].wr(), lambda v: v.tensor_copy(
                out=scr[1][:], in_=iti[:]))
            b_scr[1].did_write(t5)
            b_iti.did_read(t5)
            t6 = P.v([t5] + b_scr[3].wr(), lambda v: v.tensor_tensor(
                out=scr[3][:], in0=scr[1][:], in1=scr[0][:], op=A.is_gt))
            b_scr[3].did_write(t6)
            t7 = P.v([t6], lambda v: v.tensor_tensor(
                out=scr[1][:], in0=scr[1][:], in1=scr[3][:], op=A.subtract))
            b_scr[3].did_read(t7)
            t8 = P.v([t7], lambda v: v.tensor_scalar(
                out=scr[1][:], in0=scr[1][:], scalar1=float(emin),
                scalar2=float(p_bits - 1), op0=A.max, op1=A.subtract))
            b_scr[1].did_write(t8)
            b_scr[0].did_read(t8)
            t9 = P.s([t8] + b_scr[2].wr(), lambda s: s.activation(
                out=scr[2][:], in_=scr[1][:], func=AF.Exp, scale=LN2))
            b_scr[2].did_write(t9)
            t10 = P.s([t8] + b_scr[3].wr(), lambda s: s.activation(
                out=scr[3][:], in_=scr[1][:], func=AF.Exp, scale=-LN2))
            b_scr[3].did_write(t10)
            b_scr[1].did_read(t10)
            t11 = P.v([t10] + b_scr[0].wr(), lambda v: v.tensor_tensor(
                out=scr[0][:], in0=src_ap, in1=scr[3][:], op=A.mult))
            b_scr[0].did_write(t11)
            b_scr[3].did_read(t11)
            t12 = P.v([t11] + b_iti.wr(), lambda v: v.tensor_copy(
                out=iti[:], in_=scr[0][:]))
            b_iti.did_write(t12)
            t13 = P.v([t12] + b_scr[1].wr(), lambda v: v.tensor_copy(
                out=scr[1][:], in_=iti[:]))
            b_scr[1].did_write(t13)
            b_iti.did_read(t13)
            t14 = P.v([t13] + b_scr[3].wr(), lambda v: v.tensor_tensor(
                out=scr[3][:], in0=scr[0][:], in1=scr[1][:], op=A.subtract))
            b_scr[3].did_write(t14)
            b_scr[0].did_read(t14)
            t15 = P.v([t14], lambda v: v.scalar_tensor_tensor(
                out=scr[1][:], in0=scr[3][:], scalar=0.5, in1=scr[1][:],
                op0=A.is_gt, op1=A.add))
            t16 = P.v([t15], lambda v: v.scalar_tensor_tensor(
                out=scr[1][:], in0=scr[3][:], scalar=-0.5, in1=scr[1][:],
                op0=A.is_ge, op1=A.add))
            b_scr[1].did_write(t16)
            b_scr[3].did_read(t16)
            t17 = P.v([t16, t9] + dst_buf.wr(), lambda v: v.scalar_tensor_tensor(
                out=dst_ap, in0=scr[1][:], scalar=1.0, in1=scr[2][:],
                op0=A.subtract, op1=A.mult))
            b_scr[1].did_read(t17)
            b_scr[2].did_read(t17)
            dst_buf.did_write(t17)
            return t17

        wf_last = None
        for i in range(F):
            tld = P.y(b_scr[4].wr() + [two[-1]], lambda y, i=i: y.dma_start(
                out=scr[4][:], in_=wffn[128 * i:128 * (i + 1), :]))
            b_scr[4].did_write(tld)
            tq = q_grid(scr[4][:], scr[4][:], 4, -6, [tld], b_scr[4])
            last = None
            for f in range(F):
                bk = 6 + (f % 2)
                tt = P.t([tq, tident] + pbuf[bk].wr(), lambda t, f=f, bk=bk:
                         t.transpose(bank(bk, 128),
                                     scr[4][:, 128 * f:128 * (f + 1)], ident[:]))
                pbuf[bk].did_write(tt)
                b_scr[4].did_read(tt)
                te = P.v([tt] + b_stg.wr(), lambda v, f=f, bk=bk:
                         v.tensor_copy(
                             out=stg_hi[:, 128 * f:128 * (f + 1)],
                             in_=bank(bk, 128)))
                pbuf[bk].did_read(te)
                b_stg.did_write(te)
                last = te
            tst = P.y([last], lambda y, i=i: y.dma_start(
                out=bass.AP(wfT_d.tensor, 128 * i,
                            [[D, 128], [128 * D, F], [1, 128]]),
                in_=stg_hi))
            b_stg.did_read(tst)
            wf_last = tst
        b_wfT.did_write(wf_last)
        b_QT[0].did_read(wf_last)
        b_QT[1].did_read(wf_last)


        def ln_stats(src_ap, deps):
            ts1 = P.v(list(deps) + b_colt.wr(), lambda v: v.reduce_sum(
                out=colt[:, 0:1], in_=src_ap, axis=AX.X))
            ts2 = P.s(list(deps) + b_scr[0].wr() + b_colt.wr(),
                      lambda s: s.activation(out=scr[0][:], in_=src_ap,
                                             func=AF.Square,
                                             accum_out=colt[:, 1:2]))
            b_scr[0].did_write(ts2)
            tm = P.v([ts1], lambda v: v.tensor_scalar(
                out=colt[:, 2:3], in0=colt[:, 0:1], scalar1=1.0 / D,
                scalar2=None, op0=A.mult))
            tv1 = P.v([tm], lambda v: v.scalar_tensor_tensor(
                out=colt[:, 3:4], in0=colt[:, 2:3], scalar=-1.0,
                in1=colt[:, 2:3], op0=A.mult, op1=A.mult))
            tv2 = P.v([tv1, ts2], lambda v: v.scalar_tensor_tensor(
                out=colt[:, 3:4], in0=colt[:, 1:2], scalar=1.0 / D,
                in1=colt[:, 3:4], op0=A.mult, op1=A.add))
            tr0 = P.s([tv2], lambda s: s.activation(
                out=colt[:, 6:7], in_=colt[:, 3:4], func=AF.Sqrt,
                bias=consts[:, 0:1]))
            tr = P.v([tr0], lambda v: v.reciprocal(
                out=colt[:, 4:5], in_=colt[:, 6:7]))
            tmr = P.v([tr], lambda v: v.tensor_tensor(
                out=colt[:, 5:6], in0=colt[:, 2:3], in1=colt[:, 4:5],
                op=A.mult))
            b_colt.did_write(tmr)
            return tmr

        for b in range(NB if phases >= 2 else 0):
            base = b * S

            for t_i in range(T):
                sl = 0
                xp = x_pp[:, 0:D]
                tld = P.y(b_xpp[sl].wr(), lambda y, t_i=t_i, xp=xp: y.dma_start(
                    out=xp, in_=x_in[base + 128 * t_i:base + 128 * (t_i + 1), :]))
                b_xpp[sl].did_write(tld)
                tst = ln_stats(xp, [tld])
                tn = P.v([tst, tld] + b_scr[1].wr(), lambda v, xp=xp:
                         v.tensor_scalar(
                             out=scr[1][:], in0=xp, scalar1=colt[:, 4:5],
                             scalar2=colt[:, 5:6], op0=A.mult, op1=A.subtract))
                b_scr[1].did_write(tn)
                b_colt.did_read(tn)
                b_xpp[sl].did_read(tn)
                for f in range(F):
                    bk = 6 + (f % 2)
                    tt = P.t([tn, tident] + pbuf[bk].wr(), lambda t, f=f, bk=bk:
                             t.transpose(bank(bk, 128),
                                         scr[1][:, 128 * f:128 * (f + 1)],
                                         ident[:]))
                    pbuf[bk].did_write(tt)
                    b_scr[1].did_read(tt)
                    dh = xlnT_hi[:, S * f + 128 * t_i:S * f + 128 * (t_i + 1)]
                    dl = xlnT_lo[:, S * f + 128 * t_i:S * f + 128 * (t_i + 1)]
                    th = P.s([tt] + b_xlnT.wr(), lambda s, dh=dh, bk=bk:
                             s.activation(out=dh, in_=bank(bk, 128),
                                          func=AF.Copy))
                    tl = P.v([tt, th], lambda v, dh=dh, dl=dl, bk=bk:
                             v.tensor_tensor(out=dl, in0=bank(bk, 128), in1=dh,
                                             op=A.subtract))
                    pbuf[bk].did_read(tl)
                    b_xlnT.did_write(tl)

            tap("xlnT_hi_b%d" % b, lambda: xlnT_hi[:], [], b_xlnT,
                [128, F * S], BF16)
            tap("xlnT_lo_b%d" % b, lambda: xlnT_lo[:], [], b_xlnT,
                [128, F * S], BF16)

            if phases < 3:
                continue
            tvrow2 = P.y(b_nrow.wr(), lambda y: y.dma_start(
                out=vb_dst, in_=vb_src))
            b_nrow.did_write(tvrow2)
            tbv = P.g([tvrow2] + b_bvb.wr(), lambda g: g.partition_broadcast(
                bv_b[:], nrow[0:1, :]))
            b_bvb.did_write(tbv)
            b_nrow.did_read(tbv)
            for grp in range(16):
                tok0 = 128 * grp
                tz1 = P.v(b_KT[0].wr() + b_KT[1].wr(),
                          lambda v: v.memset(KTh[:], 0.0))
                tz2 = P.v([tz1], lambda v: v.memset(KTl[:], 0.0))
                b_KT[0].did_write(tz2)
                b_KT[1].did_write(tz2)
                for j in range(16):
                    sec, jj = divmod(j, 8)
                    slw = j % 2
                    wsrc_h = bass.AP(wqT_hi.tensor, 128 * j,
                                     [[3 * D, 128], [128 * 3 * D, F], [1, 128]])
                    wsrc_l = bass.AP(wqT_lo.tensor, 128 * j,
                                     [[3 * D, 128], [128 * 3 * D, F], [1, 128]])
                    wdst_h = qkw_hi[:, 1024 * slw:1024 * (slw + 1)].rearrange(
                        "p (f c) -> p f c", f=F)
                    wdst_l = qkw_lo[:, 1024 * slw:1024 * (slw + 1)].rearrange(
                        "p (f c) -> p f c", f=F)
                    tw1 = P.y(b_qkw[slw].wr() + [twq[-1]],
                              lambda y, a=wsrc_h, d=wdst_h:
                              y.dma_start(out=d, in_=a))
                    tw2 = P.y([tw1], lambda y, a=wsrc_l, d=wdst_l:
                              y.dma_start(out=d, in_=a))
                    b_qkw[slw].did_write(tw2)
                    b_qkw[slw].did_read(tw1)
                    bk = slw
                    mm = None
                    for f in range(F):
                        for pi, (wt, xt) in enumerate(
                                ((qkw_hi, xlnT_hi), (qkw_hi, xlnT_lo),
                                 (qkw_lo, xlnT_hi))):
                            first = (f == 0 and pi == 0)
                            last = (f == F - 1 and pi == 2)
                            lhs = wt[:, 1024 * slw + 128 * f:
                                     1024 * slw + 128 * (f + 1)]
                            rhs = xt[:, S * f + tok0:S * f + tok0 + 128]
                            mm = P.t([tw2, b_xlnT.w] +
                                     (pbuf[bk].wr() if first else []),
                                     lambda t, lhs=lhs, rhs=rhs, first=first,
                                     last=last, bk=bk: t.matmul(
                                         bank(bk, 128), lhs, rhs,
                                         start=first, stop=last))
                    pbuf[bk].did_write(mm)
                    b_qkw[slw].did_read(mm)
                    b_xlnT.did_read(mm)
                    dsts = (QTh, QTl) if sec == 0 else (KTh, KTl)
                    bufs = b_QT if sec == 0 else b_KT
                    last_e = None
                    wdeps = bufs[0].wr() + bufs[1].wr()
                    for hh in range(1):
                        for half in range(2):
                            srcp = bank(bk, 128, 64 * half, 64)
                            rowb = 64 * half
                            if sec == 0:
                                c0 = 1024 * half + 128 * jj
                            else:
                                c0 = 128 * (2 * jj + half)
                            dh = dsts[0][rowb:rowb + 64, c0:c0 + 128]
                            dl = dsts[1][rowb:rowb + 64, c0:c0 + 128]
                            bia = bq_col[64 * half:64 * (half + 1), j:j + 1]
                            th = P.s([mm] + wdeps,
                                     lambda s, srcp=srcp, dh=dh, bia=bia:
                                     s.activation(out=dh, in_=srcp,
                                                  func=AF.Identity, bias=bia))
                            tl = P.v([mm, th],
                                     lambda v, srcp=srcp, dh=dh, dl=dl, bia=bia:
                                     v.scalar_tensor_tensor(
                                         out=dl, in0=srcp, scalar=bia,
                                         op0=A.add, in1=dh, op1=A.subtract))
                            pbuf[bk].did_read(tl)
                            last_e = tl
                    bufs[0].did_write(last_e)
                    bufs[1].did_write(last_e)

                for hh in range(1):
                    for QT_ in (QTh, QTl):
                        pc = 0
                        t1 = P.y(b_QT[0].wr() + b_QT[1].wr(),
                                 lambda y, QT_=QT_, pc=pc: y.dma_start(
                                     out=QT_[64:128, pc:pc + 1024],
                                     in_=QT_[0:64, pc:pc + 1024]))
                        t2 = P.y([t1], lambda y, QT_=QT_, pc=pc: y.dma_start(
                            out=QT_[0:64, pc + 1024:pc + 2048],
                            in_=QT_[64:128, pc + 1024:pc + 2048]))
                        b_QT[0].did_write(t2)
                        b_QT[1].did_write(t2)

                mmv = {}
                for f in range(F):
                    slv = f % 2
                    tvh = P.y(b_vw2[slv].wr() + [twq[-1]],
                              lambda y, f=f, slv=slv: y.dma_start(
                                  out=vws[:, 2 * D * slv:2 * D * slv + D],
                                  in_=bass.AP(wqT_hi.tensor,
                                              f * 128 * 3 * D + 2048,
                                              [[3 * D, 128], [1, D]])))
                    tvl = P.y([tvh], lambda y, f=f, slv=slv: y.dma_start(
                        out=vws[:, 2 * D * slv + D:2 * D * (slv + 1)],
                        in_=bass.AP(wqT_lo.tensor, f * 128 * 3 * D + 2048,
                                    [[3 * D, 128], [1, D]])))
                    b_vw2[slv].did_write(tvl)
                    b_vw2[slv].did_read(tvh)
                    for hh in range(1):
                        t_i = grp
                        for half in range(2):
                            bk = 4 + half
                            for pi in range(3):
                                first = (f == 0 and pi == 0)
                                last = (f == F - 1 and pi == 2)
                                xt = xlnT_lo if pi == 1 else xlnT_hi
                                wof = D if pi == 2 else 0
                                lhs = xt[:, S * f + 128 * t_i:
                                         S * f + 128 * (t_i + 1)]
                                rhs = vws[:, 2 * D * slv + wof + 512 * half:
                                          2 * D * slv + wof + 512 * (half + 1)]
                                mm = P.t([b_xlnT.w, tvl] +
                                         (pbuf[bk].wr() if first else []),
                                         lambda t, lhs=lhs, rhs=rhs,
                                         first=first, last=last, bk=bk:
                                         t.matmul(bank(bk), lhs, rhs,
                                                  start=first, stop=last))
                                if last:
                                    pbuf[bk].did_write(mm)
                                    mmv[(hh, half)] = mm
                            b_xlnT.did_read(mm)
                            b_vw2[slv].did_read(mm)
                for hh in range(1):
                    tz = P.v(b_V[0].wr(), lambda v: v.memset(
                        Vt[:, 0:4096], 0.0))
                    b_V[0].did_write(tz)
                    tvv = None
                    for half in range(2):
                        bk = 4 + half
                        tvv = P.v([mmv[(hh, half)], tbv] + b_V[0].wr() +
                                  b_bvb.rd() + b_scr[5].wr(),
                                  lambda v, bk=bk, half=half:
                                  v.tensor_tensor(
                                      out=scr[5][:, 512 * half:
                                                512 * (half + 1)],
                                      in0=bank(bk),
                                      in1=bv_b[:, 512 * half:512 * (half + 1)],
                                      op=A.add))
                        pbuf[bk].did_read(tvv)
                        b_bvb.did_read(tvv)
                        b_scr[5].did_write(tvv)
                    tc1 = P.v([tvv] + b_x2qT.wr(), lambda v: v.tensor_copy(
                        out=x2qT[:], in_=scr[5][:]))
                    b_x2qT.did_write(tc1)
                    vdst_h = bass.AP(Vt, 0, [[4 * H * 65, 128], [128, 16], [1, 64]])
                    vdst_l = bass.AP(Vt, 2048, [[4 * H * 65, 128], [128, 16], [1, 64]])
                    th1 = P.v([tc1] + b_V[0].wr(), lambda v: v.tensor_copy(
                        out=vdst_h, in_=x2qT[:]))
                    b_V[0].did_write(th1)
                    tl1 = P.v([th1], lambda v, vdst_l=vdst_l:
                              v.scalar_tensor_tensor(
                                  out=vdst_l, in0=scr[5][:], scalar=0.0,
                                  in1=x2qT[:], op0=A.add, op1=A.subtract))
                    b_V[0].did_write(tl1)
                    b_scr[5].did_read(tl1)
                    b_x2qT.did_read(tl1)
                    tm1 = P.v(b_V[0].wr(), lambda v: v.memset(
                        bass.AP(Vt, 64, [[4 * H * 65, 128], [128, H]]), 1.0))
                    b_V[0].did_write(tm1)

                if grp == 0:
                    tap("QTh_b%d" % b, lambda: QTh[:, 0:S], [], b_QT[0], [128, S], BF16)
                    tap("QTl_b%d" % b, lambda: QTl[:, 0:S], [], b_QT[1], [128, S], BF16)
                    tap("KTh_b%d" % b, lambda: KTh[:, 0:S], [], b_KT[0], [128, S], BF16)
                    tap("KTl_b%d" % b, lambda: KTl[:, 0:S], [], b_KT[1], [128, S], BF16)
                    tap("Vt_b%d" % b, lambda: Vt[:, 0:4096], [],
                        [b_V[0]], [128, 4096], BF16)

                for pr in range(1 if phases >= 4 else 0):
                    for hx in range(1):
                        hh = 0
                        pcol = 0
                        tnc = b_negc.w
                        for qb in range(4):
                            cbk = 4 + (qb % 2)
                            exp_toks = {}
                            for m in range(9):
                                if m < 8:
                                    for ci in range(2):
                                        c = 2 * m + ci
                                        p0 = 64 * ci
                                        sbk = 2 * (m % 2) + ci
                                        mm = None
                                        for pi, (KT_, QT_) in enumerate(
                                                ((KTh, QTh), (KTh, QTl),
                                                 (KTl, QTh))):
                                            lhs = KT_[p0:p0 + 64,
                                                      pcol + 128 * c:
                                                      pcol + 128 * (c + 1)]
                                            rhs = QT_[p0:p0 + 64,
                                                      pcol + 512 * qb:
                                                      pcol + 512 * (qb + 1)]
                                            first = pi == 0
                                            lastp = pi == 2
                                            mm = P.t(
                                                [b_KT[pr].w, b_QT[pr].w] +
                                                (pbuf[sbk].wr() if first
                                                 else []),
                                                lambda t, lhs=lhs, rhs=rhs,
                                                first=first, lastp=lastp,
                                                sbk=sbk, p0=p0: t.matmul(
                                                    bank(sbk), lhs, rhs,
                                                    start=first, stop=lastp,
                                                    tile_position=(p0, 0)))
                                        pbuf[sbk].did_write(mm)
                                        b_KT[pr].did_read(mm)
                                        b_QT[pr].did_read(mm)
                                        esl = c % 4
                                        te = P.s([mm, tnc] +
                                                 b_expt[esl].wr(),
                                                 lambda s, sbk=sbk, esl=esl:
                                                 s.activation(
                                                     out=expt[:, 512 * esl:
                                                              512 * (esl + 1)],
                                                     in_=bank(sbk),
                                                     func=AF.Exp, scale=0.125,
                                                     bias=negc[:, 0:1]))
                                        b_expt[esl].did_write(te)
                                        b_negc.did_read(te)
                                        pbuf[sbk].did_read(te)
                                        teh = P.s([te] + b_eh[esl].wr(),
                                                  lambda s, esl=esl:
                                                  s.activation(
                                                      out=eh[:, 512 * esl:
                                                             512 * (esl + 1)],
                                                      in_=expt[:, 512 * esl:
                                                               512 * (esl + 1)],
                                                      func=AF.Copy))
                                        b_eh[esl].did_write(teh)
                                        tel = P.v([te, teh] + b_el[esl].wr(),
                                                  lambda v, esl=esl:
                                                  v.scalar_tensor_tensor(
                                                      out=el[:, 512 * esl:
                                                             512 * (esl + 1)],
                                                      in0=expt[:, 512 * esl:
                                                               512 * (esl + 1)],
                                                      scalar=0.0, op0=A.add,
                                                      in1=eh[:, 512 * esl:
                                                             512 * (esl + 1)],
                                                      op1=A.subtract))
                                        b_el[esl].did_write(tel)
                                        b_eh[esl].did_read(tel)
                                        b_expt[esl].did_read(tel)
                                        exp_toks[c] = tel
                                for cc in ((2 * m - 2, 2 * m - 1) if m > 0
                                           else ()):
                                    esl = cc % 4
                                    first = cc == 0
                                    lastc = cc == 15
                                    for pi, (vof, et, ebuf) in enumerate((
                                            (0, eh, b_eh),
                                            (2048, eh, b_eh),
                                            (0, el, b_el))):
                                        vap = Vt[:, vof + 128 * cc:
                                                 vof + 128 * (cc + 1)]
                                        mm = P.t(
                                            [exp_toks[cc], b_V[0].w] +
                                            (pbuf[cbk].wr()
                                             if (first and pi == 0)
                                             else []),
                                            lambda t, vap=vap, esl=esl,
                                            et=et,
                                            st_=(first and pi == 0),
                                            sp_=(lastc and pi == 2),
                                            cbk=cbk: t.matmul(
                                                bank(cbk), vap,
                                                et[:, 512 * esl:
                                                   512 * (esl + 1)],
                                                start=st_, stop=sp_))
                                        pbuf[cbk].did_write(mm)
                                        ebuf[esl].did_read(mm)
                                        b_V[0].did_read(mm)
                            tk_mm = pbuf[cbk].w
                            trc = P.v([tk_mm] + b_cn.wr(), lambda v,
                                      cbk=cbk: v.reciprocal(
                                          out=cn_s[64:65, :],
                                          in_=bank(cbk, 512, 64, 1)))
                            b_cn.did_write(trc)
                            tbm = P.t([trc, tonesr] + pbuf[6].wr(),
                                      lambda t: t.matmul(
                                          bank(6), onesr[64:65, 0:128],
                                          cn_s[64:65, :], start=True,
                                          stop=True, tile_position=(64, 0)))
                            pbuf[6].did_write(tbm)
                            b_cn.did_read(tbm)
                            tbc = P.v([tbm] + b_rcb.wr(),
                                      lambda v: v.tensor_copy(
                                          out=rcb[:],
                                          in_=bank(6, 512, 0, 64)))
                            b_rcb.did_write(tbc)
                            pbuf[6].did_read(tbc)
                            tcn = P.v([tbc, tk_mm] + b_cn.wr(),
                                      lambda v, cbk=cbk: v.tensor_tensor(
                                          out=cn_s[0:64, :],
                                          in0=bank(cbk, 512, 0, 64),
                                          in1=rcb[:], op=A.mult))
                            b_cn.did_write(tcn)
                            b_rcb.did_read(tcn)
                            pbuf[cbk].did_read(tcn)
                            if grp == 0 and hx == 0 and qb == 0:
                                tap("cn_b%d" % b, lambda: cn_s[0:65, :], [tcn],
                                    b_cn, [65, 512], F32)
                                tap("exp_b%d" % b, lambda: expt[:],
                                    [tcn], list(b_expt), [128, 4 * 512], F32)
                                if ("ctxp_b%d" % b) in taps:
                                    tdbg = P.v([tcn] + b_nrm.wr(),
                                               lambda v: v.tensor_copy(
                                                   out=nrm[0:65, :],
                                                   in_=bank(cbk, 512, 0, 65)))
                                    b_nrm.did_write(tdbg)
                                    pbuf[cbk].did_read(tdbg)
                                    tap("ctxp_b%d" % b,
                                        lambda: nrm[0:65, :], [tdbg],
                                        b_nrm, [65, 512], BF16)
                                tap("eh_b%d" % b, lambda: eh[:], [tcn],
                                    list(b_eh), [128, 4 * 512], BF16)
                                tap("el_b%d" % b, lambda: el[:], [tcn],
                                    list(b_el), [128, 4 * 512], BF16)
                            if grp % 2 == 1:
                                tmv = P.y([tcn] + b_cn.wr(), lambda y:
                                          y.dma_start(out=cn_s[64:128, :],
                                                      in_=cn_s[0:64, :]))
                                b_cn.did_write(tmv)
                                src_half = cn_s[64:128, :]
                            else:
                                src_half = cn_s[0:64, :]
                            rowb = 64 * (grp % 2)
                            cbase = S * rowb + 8 * (qb % 2) + (qb // 2)
                            dhi = bass.AP(cts_hi, cbase,
                                          [[S, 64], [2, 4], [16, 128]])
                            dlo = bass.AP(cts_lo, cbase,
                                          [[S, 64], [2, 4], [16, 128]])
                            shp = src_half.rearrange("p (g r) -> p g r", g=4)
                            thv = P.v(b_cn.rd() + b_cts.wr(),
                                      lambda v, dhi=dhi, shp=shp:
                                      v.tensor_copy(out=dhi, in_=shp))
                            tlv = P.v([thv], lambda v, dlo=dlo, dhi=dhi,
                                      shp=shp: v.tensor_tensor(
                                          out=dlo, in0=shp, in1=dhi,
                                          op=A.subtract))
                            b_cn.did_read(tlv)
                            b_cn.did_read(thv)
                            b_cts.did_write(tlv)

                    if grp % 2 == 0:
                        continue
                    jcf = grp // 2
                    tso = P.y(b_cts.wr() + b_ctd.rd(), lambda y, jcf=jcf:
                              y.dma_start(out=bass.AP(ct_dhi.tensor,
                                                      jcf * 128 * S,
                                                      [[S, 128], [1, S]]),
                                          in_=cts_hi[:]))
                    tso2 = P.y([tso], lambda y, jcf=jcf:
                               y.dma_start(out=bass.AP(ct_dlo.tensor,
                                                       jcf * 128 * S,
                                                       [[S, 128], [1, S]]),
                                           in_=cts_lo[:]))
                    b_cts.did_read(tso2)
                    b_cts.did_read(tso)
                    b_ctd.did_write(tso2)

            tap("ctd_hi_b%d" % b,
                lambda: bass.AP(ct_dhi.tensor, 0, [[128 * S, F], [S, 128], [1, S]]),
                [], b_ctd, [F, 128, S], BF16)
            tap("ctd_lo_b%d" % b,
                lambda: bass.AP(ct_dlo.tensor, 0, [[128 * S, F], [S, 128], [1, S]]),
                [], b_ctd, [F, 128, S], BF16)

            if phases < 5:
                continue
            tbo_row = P.y(b_nrow.wr(), lambda y: y.dma_start(
                out=nrow[0:1, :], in_=bass.AP(bout.tensor, 0, [[1, 1], [1, D]])))
            b_nrow.did_write(tbo_row)
            tbo = P.g([tbo_row] + b_bvb.wr(), lambda g: g.partition_broadcast(
                bo_b[:], nrow[0:1, :]))
            b_bvb.did_write(tbo)
            b_nrow.did_read(tbo)
            for sg in range(4):
                mm_fin = []
                for cf in range(F):
                    slw = 0
                    tws = P.y(b_ws[slw].wr() + [two[-1]], lambda y, cf=cf,
                              slw=slw: y.dma_start(
                                  out=wo_s[:, 2 * D * slw:2 * D * slw + D],
                                  in_=bass.AP(woT_hi.tensor, cf * 128 * D,
                                              [[D, 128], [1, D]])))
                    tws2 = P.y([tws], lambda y, cf=cf, slw=slw: y.dma_start(
                        out=wo_s[:, 2 * D * slw + D:2 * D * (slw + 1)],
                        in_=bass.AP(woT_lo.tensor, cf * 128 * D,
                                    [[D, 128], [1, D]])))
                    b_ws[slw].did_write(tws2)
                    b_ws[slw].did_read(tws)
                    tcs = P.y(b_cs[slw].wr() + b_ctd.rd(), lambda y, cf=cf,
                              slw=slw, sg=sg: y.dma_start(
                                  out=ct_s[:, 1024 * slw:1024 * slw + 512],
                                  in_=bass.AP(ct_dhi.tensor,
                                              cf * 128 * S + 512 * sg,
                                              [[S, 128], [1, 512]])))
                    tcs2 = P.y([tcs], lambda y, cf=cf, slw=slw, sg=sg:
                               y.dma_start(
                                   out=ct_s[:, 1024 * slw + 512:
                                            1024 * (slw + 1)],
                                   in_=bass.AP(ct_dlo.tensor,
                                               cf * 128 * S + 512 * sg,
                                               [[S, 128], [1, 512]])))
                    b_cs[slw].did_write(tcs2)
                    b_cs[slw].did_read(tcs)
                    b_ctd.did_read(tcs2)
                    for tau in range(4):
                        for half in range(2):
                            bk = 2 * tau + half
                            for pi in range(3):
                                first = (cf == 0 and pi == 0)
                                lastp = (cf == F - 1 and pi == 2)
                                lof = 0 if pi < 2 else 512
                                wof = D if pi == 1 else 0
                                lhs = ct_s[:, 1024 * slw + lof + 128 * tau:
                                           1024 * slw + lof + 128 * (tau + 1)]
                                rhs = wo_s[:, 2 * D * slw + wof + 512 * half:
                                           2 * D * slw + wof + 512 * (half + 1)]
                                mm = P.t([tws2, tcs2] +
                                         (pbuf[bk].wr() if first else []),
                                         lambda t, lhs=lhs, rhs=rhs,
                                         first=first, lastp=lastp, bk=bk:
                                         t.matmul(bank(bk), lhs, rhs,
                                                  start=first, stop=lastp))
                                if lastp:
                                    pbuf[bk].did_write(mm)
                                    mm_fin.append(mm)
                            b_cs[slw].did_read(mm)
                            b_ws[slw].did_read(mm)
                    if cf < F - 1:
                        pass

                for tau in range(4):
                    t_i = 4 * sg + tau
                    sl = 0
                    xp = x_pp[:, 0:D]
                    tld = P.y(b_xpp[sl].wr(), lambda y, t_i=t_i, xp=xp:
                              y.dma_start(out=xp,
                                          in_=x_in[base + 128 * t_i:
                                                   base + 128 * (t_i + 1), :]))
                    b_xpp[sl].did_write(tld)
                    tev = None
                    for half in range(2):
                        bk = 2 * tau + half
                        tev = P.v([pbuf[bk].w, tbo] + b_scr[5].wr() + b_bvb.rd(),
                                  lambda v, bk=bk, half=half: v.tensor_tensor(
                                      out=scr[5][:, 512 * half:512 * (half + 1)],
                                      in0=bank(bk),
                                      in1=bo_b[:, 512 * half:512 * (half + 1)],
                                      op=A.add))
                        pbuf[bk].did_read(tev)
                        b_bvb.did_read(tev)
                        b_scr[5].did_write(tev)
                    tst = ln_stats(scr[5][:], [tev])
                    tn2 = P.v([tst] + b_scr[1].wr(), lambda v: v.tensor_scalar(
                        out=scr[1][:], in0=scr[5][:], scalar1=colt[:, 4:5],
                        scalar2=colt[:, 5:6], op0=A.mult, op1=A.subtract))
                    b_scr[1].did_write(tn2)
                    b_scr[5].did_read(tn2)
                    b_colt.did_read(tn2)
                    tx1 = P.v([tn2, tld] + b_xpp[sl].wr(), lambda v, xp=xp:
                              v.tensor_tensor(out=xp, in0=xp,
                                              in1=scr[1][:], op=A.add))
                    b_xpp[sl].did_write(tx1)
                    b_x1.did_write(tx1)
                    b_scr[1].did_read(tx1)
                    tst3 = ln_stats(xp, [tx1])
                    tn3 = P.v([tst3] + b_scr[4].wr(), lambda v, xp=xp:
                              v.tensor_scalar(
                                  out=scr[4][:], in0=xp, scalar1=colt[:, 4:5],
                                  scalar2=colt[:, 5:6], op0=A.mult,
                                  op1=A.subtract))
                    b_scr[4].did_write(tn3)
                    b_colt.did_read(tn3)
                    b_x1.did_read(tn3)
                    tq = q_grid(scr[4][:], scr[4][:], 4, -6, [tn3], b_scr[4])
                    tevq = None
                    for f in range(F):
                        bk = 2 * tau
                        bk = bk + (f % 2)
                        tt = P.t([tq, tident] + pbuf[bk].wr(),
                                 lambda t, f=f, bk=bk: t.transpose(
                                     bank(bk, 128),
                                     scr[4][:, 128 * f:128 * (f + 1)],
                                     ident[:]))
                        pbuf[bk].did_write(tt)
                        b_scr[4].did_read(tt)
                        tevq = P.v([tt] + (b_x2qT.wr() if f == 0 else []),
                                   lambda v, f=f, bk=bk: v.tensor_copy(
                                       out=x2qT[:, 128 * f:128 * (f + 1)],
                                       in_=bank(bk, 128)))
                        pbuf[bk].did_read(tevq)
                    b_x2qT.did_write(tevq)
                    mmf = {}
                    for cf in range(F):
                        slf = 0
                        twf = P.y(b_wfs[slf].wr() + [wf_last],
                                  lambda y, cf=cf, slf=slf: y.dma_start(
                                      out=wfs[:, D * slf:D * (slf + 1)],
                                      in_=bass.AP(wfT_d.tensor, cf * 128 * D,
                                                  [[D, 128], [1, D]])))
                        b_wfs[slf].did_write(twf)
                        for half in range(2):
                            bk = 2 * tau + half
                            first = cf == 0
                            lastc = cf == F - 1
                            lhs = x2qT[:, 128 * cf:128 * (cf + 1)]
                            rhs = wfs[:, D * slf + 512 * half:
                                      D * slf + 512 * (half + 1)]
                            mm = P.t([tevq, twf] +
                                     (pbuf[bk].wr() if first else []),
                                     lambda t, lhs=lhs, rhs=rhs, first=first,
                                     lastc=lastc, bk=bk: t.matmul(
                                         bank(bk), lhs, rhs,
                                         start=first, stop=lastc))
                            b_wfs[slf].did_read(mm)
                            b_x2qT.did_read(mm)
                            mmf[half] = mm
                            if lastc:
                                pbuf[bk].did_write(mm)
                    for half in range(2):
                        bk = 2 * tau + half
                        tevf = P.v([mmf[half], tbf] + b_scr[5].wr(),
                                   lambda v, bk=bk, half=half: v.tensor_tensor(
                                       out=scr[5][:, 512 * half:
                                                 512 * (half + 1)],
                                       in0=bank(bk),
                                       in1=bf_b[:, 512 * half:512 * (half + 1)],
                                       op=A.add))
                        pbuf[bk].did_read(tevf)
                        b_scr[5].did_write(tevf)
                    tq2 = q_grid(scr[5][:], scr[5][:], 11, -14,
                                 b_scr[5].wr(), b_scr[5])
                    tst4 = ln_stats(scr[5][:], [tq2])
                    tn4 = P.v([tst4] + b_scr[1].wr(), lambda v: v.tensor_scalar(
                        out=scr[1][:], in0=scr[5][:], scalar1=colt[:, 4:5],
                        scalar2=colt[:, 5:6], op0=A.mult, op1=A.subtract))
                    b_scr[1].did_write(tn4)
                    b_scr[5].did_read(tn4)
                    b_colt.did_read(tn4)
                    tout = P.v([tn4, tx1] + b_scr[2].wr(), lambda v, xp=xp:
                               v.tensor_tensor(out=scr[2][:], in0=xp,
                                               in1=scr[1][:], op=A.add))
                    b_scr[2].did_write(tout)
                    b_x1.did_read(tout)
                    b_xpp[sl].did_read(tout)
                    b_scr[1].did_read(tout)
                    tst_o = P.y([tout], lambda y, t_i=t_i: y.dma_start(
                        out=out_d[base + 128 * t_i:base + 128 * (t_i + 1), :],
                        in_=scr[2][:]))
                    b_scr[2].did_read(tst_o)
                    ftoks.append(tst_o)

        def fin(y):
            best = {}
            for sem, val in ftoks:
                if best.get(id(sem), (None, 0))[1] < val:
                    best[id(sem)] = (sem, val)
            for sem, val in best.values():
                y.wait_ge(sem, val)
        P.blk.sync(fin)

    mybir.codegen_inst_isa_subclasses(nc)
    return nc, tap_aps


def kernel(**inputs):
    x = np.ascontiguousarray(np.asarray(inputs["L0_attn_x"], dtype=np.float32))
    w_qkv = np.ascontiguousarray(np.asarray(inputs["L0_attn_W_qkv"], np.float32))
    b_qkv = np.ascontiguousarray(np.asarray(inputs["L0_attn_b_qkv"], np.float32))
    w_out = np.ascontiguousarray(np.asarray(inputs["L0_attn_W_out"], np.float32))
    b_out = np.ascontiguousarray(np.asarray(inputs["L0_attn_b_out"], np.float32))
    w_ffn = np.ascontiguousarray(np.asarray(inputs["L0_ffn_weight"], np.float32))
    b_ffn = np.ascontiguousarray(np.asarray(inputs["L0_ffn_bias"], np.float32))

    nc, _ = build_program()
    nonce = [t for t in (a.memorylocations[0].name
                         for a in nc.m.functions[0].allocations
                         if hasattr(a, "memorylocations") and a.memorylocations)
             if str(t).startswith("nonce_")]
    in_maps = []
    for core in range(8):
        xs = x[NB * core:NB * (core + 1)].reshape(NB * S, D)
        m = {
            "x": np.ascontiguousarray(xs),
            "w_qkv": w_qkv, "b_qkv": b_qkv,
            "w_out": w_out, "b_out": b_out,
            "w_ffn": w_ffn, "b_ffn": b_ffn,
        }
        for nn in nonce:
            m[nn] = np.zeros((1, 1), np.float32)
        in_maps.append(m)
    res = run_bass_kernel_spmd(nc, in_maps, core_ids=list(range(8)))
    out = np.concatenate([r["out"].reshape(NB, S, D) for r in res.results],
                         axis=0)
    return out
